# version 07
# speedup vs baseline: 1.1795x; 1.0350x over previous
import numpy as np
import ml_dtypes
import concourse.bass as bass
import concourse.mybir as mybir
from concourse.bass_utils import run_bass_kernel_spmd

F32 = mybir.dt.float32
BF16 = mybir.dt.bfloat16
I32 = mybir.dt.int32
AF = mybir.ActivationFunctionType
ALU = mybir.AluOpType
AX = mybir.AxisListType
BF = ml_dtypes.bfloat16

D = 2048
T = 2048
NCORE = 8
H = 8
DK = 256
NC = D // 128
TT = 256
NTILE = T // TT
STT = TT // 128
W = 128
NW = TT // W
INTER = 2816
NKD = INTER // 128
NMG = 2 * INTER // 128
EPS = 1e-5
GRID = 16
DSL = 32
BIG = 1.0e30
GAMMA = [float(1.0 - 2.0 ** (-5.0 - h)) for h in range(H)]


def host_constants():
    c = {}
    invf = (1.0 / 10000.0 ** (np.arange(0, DK, 2, dtype=np.float32) / DK)).astype(np.float32)
    fr = np.arange(T, dtype=np.float32)[:, None] * invf[None, :]
    c["cos_t"] = np.ascontiguousarray(np.cos(fr).T).astype(np.float32)
    c["sin_t"] = np.ascontiguousarray(np.sin(fr).T).astype(np.float32)
    dm = np.zeros((128, H * W), np.float64)
    qs = np.zeros((1, H * W), np.float64)
    ks = np.zeros((128, H), np.float64)
    for h in range(H):
        g = GAMMA[h]
        col = g ** (-np.arange(W, dtype=np.float64))
        dm[:, h * W:(h + 1) * W] = np.where(
            np.arange(W)[None, :] >= np.arange(W)[:, None], col[:, None], 0.0)
        qs[0, h * W:(h + 1) * W] = (g ** np.arange(W, dtype=np.float64)) * (DK ** -0.5)
        ks[:, h] = g ** (W - np.arange(W, dtype=np.float64))
    c["dm_t"] = dm.astype(np.float32)
    c["qsc_row"] = qs.astype(np.float32)
    c["ksc"] = ks.astype(np.float32)
    c["ones_1x128"] = np.ones((1, 128), np.float32)
    c["ones_128x1"] = np.ones((128, 1), np.float32)
    c["ident_bf"] = np.eye(128, dtype=np.float32).astype(BF)
    c["ident_f"] = np.eye(128, dtype=np.float32)
    i64 = np.eye(64, dtype=np.float32)
    c["ident64d"] = np.concatenate([i64, i64], axis=0)
    sel8 = np.zeros((128, NC * 8), np.float32)
    for m in range(NC):
        sel8[:, m * 8 + m // 2] = 1.0
    c["sel8"] = sel8
    selh = np.zeros((8, H * 128), np.float32)
    for h in range(H):
        selh[h, h * 128:(h + 1) * 128] = 1.0
    c["selh"] = selh
    sc = np.zeros((128, NC * 64), np.float32)
    for k in range(NC):
        for dl in range(128):
            sc[dl, k * 64 + 4 * k + dl // 32] = 1.0
    c["selsc"] = sc.astype(BF)
    rf = np.zeros((128, 32), np.float32)
    for a in range(4):
        rf[32 * a + np.arange(32), np.arange(32)] = 1.0
    c["rfold"] = rf
    ek = np.zeros((64, NC * 128), np.float32)
    for k in range(NC):
        for a in range(4):
            ek[4 * k + a, k * 128 + 32 * a: k * 128 + 32 * a + 32] = 1.0
    c["ek2"] = np.concatenate([ek, ek], axis=0).astype(BF)
    pr = np.zeros((32, 128), np.float32)
    for a in range(4):
        pr[np.arange(32), 32 * a + np.arange(32)] = 1.0
    c["prep"] = pr
    c["iota64"] = np.tile(np.arange(64, dtype=np.float32), (128, 1))
    c["riota64"] = np.tile(63.0 - np.arange(64, dtype=np.float32), (128, 1))
    c["iotac64"] = np.tile((np.arange(64) // 8).astype(np.float32), (128, 1))
    c["riota8"] = np.tile(7.0 - np.arange(8, dtype=np.float32), (128, 1))
    gp = np.zeros((128, 4), np.float32)
    for g4 in range(4):
        gp[:, g4] = g4 * 4 + (np.arange(128) // 32)
    c["gpp4"] = gp
    c["grow"] = np.tile(np.arange(GRID, dtype=np.float32), (1, DSL))
    c["cvals"] = np.full((128, 1), EPS, np.float32)
    th = np.zeros((128, 4), np.float32)
    for it in range(4):
        th[:, it] = it * 4 + (np.arange(128) // 32) + 1
    c["gith"] = th
    return c


def diag_layout(conv_w):
    out = np.zeros((NC * 4 * 128, 128), np.float32)
    for m in range(NC):
        for i in range(4):
            b = (m * 4 + i) * 128
            out[b + np.arange(128), np.arange(128)] = conv_w[m * 128:(m + 1) * 128, i]
    return out


class Prog:
    ENGS = ("sync", "tensor", "vector", "scalar", "gpsimd")

    def __init__(self):
        self.items = {e: [] for e in self.ENGS}
        self.count = {}

    def op(self, eng, fn, waits=(), inc=None, inc_n=1):
        item = {"fn": fn, "waits": [(s, v) for (s, v) in waits if v > 0],
                "inc": inc, "inc_n": inc_n}
        if inc is not None:
            self.count[inc] = self.count.get(inc, 0) + inc_n
        self.items[eng].append(item)
        return self.count.get(inc)

    def now(self, sem):
        return self.count.get(sem, 0)

    DRAIN_THRESH = 16384

    def emit(self, nc, sems):
        block = nc.Block()
        block.__enter__()
        thresh = self.DRAIN_THRESH

        class _Prox:
            def __init__(self, eng):
                self._e = eng
                self._need_drain = False

            def __getattr__(self, name):
                real = getattr(self._e, name)
                if not callable(name) and not callable(real):
                    return real

                def wrap(*a, **k):
                    if name in ("tensor_reduce", "reduce_max", "reduce_sum",
                                "reciprocal", "max", "bn_stats", "bn_aggr"):
                        self._need_drain = True
                    elif a and hasattr(a[0], "size"):
                        try:
                            if a[0].size() < thresh:
                                self._need_drain = True
                        except Exception:
                            pass
                    return real(*a, **k)
                return wrap

        def runner(name):
            def run(eng):
                marks = {}
                use_prox = name in ("vector", "scalar")
                for it in self.items[name]:
                    for (s, v) in it["waits"]:
                        if v > marks.get(s, 0):
                            eng.wait_ge(sems[s], v)
                            marks[s] = v
                    if use_prox:
                        prox = _Prox(eng)
                        inst = it["fn"](prox)
                        if prox._need_drain:
                            inst = eng.drain()
                    else:
                        inst = it["fn"](eng)
                    if it["inc"] is not None:
                        assert inst is not None
                        inst.then_inc(sems[it["inc"]], it["inc_n"])
            return run

        block.sync(runner("sync"))
        block.tensor(runner("tensor"))
        block.vector(runner("vector"))
        block.scalar(runner("scalar"))
        block.gpsimd(runner("gpsimd"))
        block.__exit__(None, None, None)


def build(stage=9, taps=()):
    nc = bass.Bass("TRN2", target_bir_lowering=False, detect_race_conditions=False)
    P = Prog()
    taps = set(taps)
    from contextlib import ExitStack
    es = ExitStack()

    def din(name, shape, dt=F32):
        return nc.dram_tensor(name, list(shape), dt, kind="ExternalInput").ap()

    x_fm = din("x_fm", (D, T))
    v_w = {n: din(n, (D, D)) for n in ("wq", "wk", "wv", "wg", "wo")}
    v_wgate = din("wgate", (D, 2 * INTER)); v_wdown = din("wdown", (INTER, D))
    v_diags = din("diags", (3 * NC * 4 * 128, 128))
    attn_nw = din("attn_nw", (1, D)); gnorm_w = din("gnorm_w", (1, DK))
    mlp_nw = din("mlp_nw", (1, D))
    ln_g = din("ln_g", (1, D)); ln_b = din("ln_b", (1, D))
    spl_b = din("spl_b", (64, DSL * GRID)); spl_s = din("spl_s", (64, DSL * GRID))
    osc_dup = din("osc_dup", (128, 1))
    c_cos = din("cos_t", (128, T)); c_sin = din("sin_t", (128, T))
    c_dm = din("dm_t", (128, H * W))
    c_qsc = din("qsc_row", (1, H * W)); c_ksc = din("ksc", (128, H))
    c_ones1 = din("ones_1x128", (1, 128)); c_ones128 = din("ones_128x1", (128, 1))
    c_identb = din("ident_bf", (128, 128), BF16)
    c_identf = din("ident_f", (128, 128))
    c_ident64d = din("ident64d", (128, 64))
    c_sel8 = din("sel8", (128, NC * 8)); c_selh = din("selh", (8, H * 128))
    c_selsc = din("selsc", (128, NC * 64), BF16)
    c_rfold = din("rfold", (128, 32)); c_ek2 = din("ek2", (128, NC * 128), BF16)
    c_prep = din("prep", (32, 128))
    c_iota64 = din("iota64", (128, 64)); c_riota64 = din("riota64", (128, 64))
    c_iotac64 = din("iotac64", (128, 64)); c_riota8 = din("riota8", (128, 8))
    c_gpp4 = din("gpp4", (128, 4)); c_grow = din("grow", (1, DSL * GRID))
    c_cvals = din("cvals", (128, 1))
    c_gith = din("gith", (128, 4))

    out_fm = nc.dram_tensor("out_fm", [D, T], F32, kind="ExternalOutput").ap()
    tap_t = {}

    def mktap(name, shape, dt=F32):
        if name in taps:
            tap_t[name] = nc.dram_tensor("tap_" + name, list(shape), dt,
                                         kind="ExternalOutput").ap()

    for name, shape in [
        ("xn", (D, T)), ("xf", (D, T)), ("at", (D, T)), ("sg", (D, T)),
        ("qr", (D, T)), ("kr", (D, T)), ("vt", (T, D)), ("kt", (T, D)),
        ("og", (D, T)), ("hh", (D, T)), ("ym", (D, T)), ("mlpact", (INTER, T)),
        ("sfin", (D, T)), ("scores", (64, T)), ("onehot", (128, T // 2)),
        ("xsl", (64, T // 2)), ("gmm", (128, 8)), ("u16", (64, T)),
        ("delta", (64, T)), ("gffn", (D, T)), ("stats", (NTILE, 5 * TT)),
        ("sctm", (128, T // 128 * 64)), ("amx", (128, T // 128 * 12)),
        ("m1t", (128, DSL * GRID)), ("m2t", (128, DSL * GRID)),
        ("acc1", (128, T)), ("acc2", (128, T)), ("girep", (128, T)),
        ("ts32", (128, H * 512)),
    ]:
        mktap(name, shape)

    wbname = {"wq": "wq_b", "wk": "wk_b", "wv": "wv_b", "wg": "wg_b", "wo": "wo_b"}
    w_b = {n: nc.dram_tensor(wbname[n], [D, D], BF16).ap() for n in v_w}
    wgate_b = nc.dram_tensor("wgate_b", [D, 2 * INTER], BF16).ap()
    wdown_b = nc.dram_tensor("wdown_b", [INTER, D], BF16).ap()
    diags_b = nc.dram_tensor("diags_b", [3 * NC * 4 * 128, 128], BF16).ap()
    ss_dram = nc.dram_tensor("ss_dram", [1, D], F32).ap()
    stat_dram = nc.dram_tensor("stat_dram", [NTILE, 5 * TT], F32).ap()
    cc_in = nc.dram_tensor("cc_in", [128, 128], F32)
    cc_out = nc.dram_tensor("cc_out", [128, 128], F32)

    def sb(name, shape, dt=F32):
        return es.enter_context(nc.sbuf_tensor(name, list(shape), dt))

    s_cos = sb("s_cos", (128, T)); s_sin = sb("s_sin", (128, T))
    s_dm = sb("s_dm", (128, H * W))
    s_qsc = sb("s_qsc", (128, H * W), BF16)
    s_ksc = sb("s_ksc", (128, H))
    s_identb = sb("s_identb", (128, 128), BF16)
    s_identf = sb("s_identf", (128, 128))
    s_id64 = sb("s_id64", (128, 64))
    s_ones1 = sb("s_ones1", (1, 128))
    s_ones128 = sb("s_ones128", (128, 1))
    s_sel8 = sb("s_sel8", (128, NC * 8)); s_selh = sb("s_selh", (8, H * 128))
    s_selsc = sb("s_selsc", (128, NC * 64), BF16)
    s_rfold = sb("s_rfold", (128, 32)); s_ek2 = sb("s_ek2", (128, NC * 128), BF16)
    s_prep = sb("s_prep", (32, 128))
    s_iota64 = sb("s_iota64", (128, 64)); s_riota64 = sb("s_riota64", (128, 64))
    s_iotac64 = sb("s_iotac64", (128, 64)); s_riota8 = sb("s_riota8", (128, 8))
    s_gpp4 = sb("s_gpp4", (128, 4)); s_grow = sb("s_grow", (1, DSL * GRID))
    s_cv = sb("s_cv", (128, 1))
    s_gith = sb("s_gith", (128, 4))
    nc.const_aps.register(F32, EPS, s_cv[:, 0:1])
    s_attnw = sb("s_attnw", (128, NC)); s_mlpw = sb("s_mlpw", (128, NC))
    s_lng = sb("s_lng", (128, NC)); s_lnb = sb("s_lnb", (128, NC))
    s_gnw = sb("s_gnw", (128, 2))
    s_osc = sb("s_osc", (128, 1))
    s_ssf = sb("s_ssf", (128, NC))
    s_gssf = sb("s_gssf", (128, NC)); s_bssf = sb("s_bssf", (128, NC))
    s_m1 = sb("s_m1", (128, DSL * GRID)); s_m2 = sb("s_m2", (128, DSL * GRID))
    s_xnh = sb("s_xnh", (128, NC * TT))
    s_u1 = sb("s_u1", (128, NC * TT))
    s_u2 = sb("s_u2", (128, 2 * NC * TT), BF16)
    s_at = sb("s_at", (128, NC * TT), BF16)
    s_sg = sb("s_sg", (128, NC * TT), BF16)
    s_t32 = sb("s_t32", (128, H * 512))
    s_tbf = sb("s_tbf", (128, H * 512), BF16)
    s_kt = sb("s_kt", (128, D), BF16)
    s_as = sb("s_as", (128, H * W), BF16)
    s_wb = sb("s_wb", (128, 8192), BF16)
    s_xc = sb("s_xc", (128, 2 * TT))
    s_oc = sb("s_oc", (128, 2 * TT))
    s_q0e = sb("s_q0e", (128, 2 * (TT + 4)), BF16)
    s_halo = sb("s_halo", (128, 3 * NC * 3), BF16)
    s_scr = sb("s_scr", (128, 2 * TT))
    s_scr2 = sb("s_scr2", (128, 2 * TT))
    s_rb = sb("s_rb", (128, H * 128))
    s_row = sb("s_row", (1, 10 * TT))
    s_oht = sb("s_oht", (128, T // 2))
    s_ohtb = sb("s_ohtb", (128, T // 2), BF16)
    s_xsl = sb("s_xsl", (64, T // 2))
    s_tmm = sb("s_tmm", (128, 8))
    s_gmm = sb("s_gmm", (128, 8))
    s_sc_tm = sb("s_sc_tm", (128, STT * 64), BF16)
    s_rstd = sb("s_rstd", (8, W))
    s_u16 = sb("s_u16", (64, TT))
    s_gif = sb("s_gif", (64, TT))
    s_m1g = sb("s_m1g", (128, 512)); s_m2g = sb("s_m2g", (128, 512))
    s_msk = sb("s_msk", (128, TT))
    s_dl = sb("s_dl", (128, TT))
    s_gch = sb("s_gch", (128, 2 * TT))
    s_cc = sb("s_cc", (128, 128))

    def psum(name, shape):
        return es.enter_context(nc.psum_tensor(name, list(shape), F32))

    pp = [psum(f"pp{i}", (128, 512)) for i in range(4)]
    pA = psum("pA", (128, 1024))
    pD = psum("pD", (128, 512))
    pS = psum("pS", (128, 512))
    ppb = [p[:, :].bitcast(BF16) for p in pp]
    pDb = pD[:, :].bitcast(BF16)

    semnames = ["wc", "wxt", "wx0", "wx1", "ww0", "ww1", "ww2", "ww3", "wg0", "wg1", "wg2", "wg3", "wr",
                "ci0", "ci1", "ci2", "co0", "co1", "co2", "gd", "od0", "od1", "cc",
                "pe", "dv",
                "ac", "gp"]
    sems = {n: es.enter_context(nc.semaphore(n)) for n in semnames}

    OP = P.op
    N = P.now

    U1B = s_u1[:, :].bitcast(BF16)

    def XNc(m, t0=0, n=TT):
        return s_xnh[:, m * TT + t0:m * TT + t0 + n]

    def XFc(m, t0=0, n=TT):
        return s_u1[:, m * TT + t0:m * TT + t0 + n]

    def QF(m, t0=0, n=TT):
        return U1B[:, m * TT + t0:m * TT + t0 + n]

    def KF(m, t0=0, n=TT):
        return U1B[:, NC * TT + m * TT + t0:NC * TT + m * TT + t0 + n]

    def MLPA(m, t0=0, n=TT):
        return U1B[:, m * TT + t0:m * TT + t0 + n]

    def VT(w, dlo, n):
        return s_u2[:, w * D + dlo:w * D + dlo + n]

    def ATCH(m, t0=0, n=TT):
        return s_at[:, m * TT + t0:m * TT + t0 + n]

    def SGCH(m, t0=0, n=TT):
        return s_sg[:, m * TT + t0:m * TT + t0 + n]

    def T32A(h, a):
        return s_t32[:, h * 512 + a * 256:h * 512 + a * 256 + 256]

    def TBF(h, a, e0=0, n=256):
        return s_tbf[:, h * 512 + a * 256 + e0:h * 512 + a * 256 + e0 + n]

    def R(i):
        return s_row[0:1, i * TT:(i + 1) * TT]

    def tapout(name, dst_slice, src_ap, waits, eng="gpsimd"):
        if name in taps:
            OP(eng, lambda e, d=dst_slice, s=src_ap: e.dma_start(out=d, in_=s),
               waits=list(waits) + [("gd", N("gd"))], inc="gd", inc_n=16)
            OP("vector", lambda e: e.memset(s_cc[0:1, 2:3], 0.0),
               waits=[("gd", N("gd"))], inc="dv")
            OP("scalar", lambda e: e.activation(s_cc[0:1, 3:4], s_cc[0:1, 2:3], AF.Copy),
               waits=[("gd", N("gd")), ("dv", N("dv"))], inc="ac")

    for src, dst in [
        (c_cos, s_cos), (c_sin, s_sin), (c_dm, s_dm), (c_ksc, s_ksc),
        (c_identb, s_identb), (c_identf, s_identf), (c_ident64d, s_id64),
        (c_ones1, s_ones1),
        (c_ones128, s_ones128), (c_sel8, s_sel8), (c_selh, s_selh),
        (c_selsc, s_selsc), (c_rfold, s_rfold), (c_ek2, s_ek2), (c_prep, s_prep),
        (c_iota64, s_iota64), (c_riota64, s_riota64), (c_iotac64, s_iotac64),
        (c_riota8, s_riota8), (c_gpp4, s_gpp4), (c_grow, s_grow), (osc_dup, s_osc),
        (c_cvals, s_cv), (c_gith, s_gith),
    ]:
        OP("sync", lambda e, s=src, d=dst: e.dma_start(out=d[:, :], in_=s),
           inc="wc", inc_n=16)
    for src, dst, n in [(attn_nw, s_attnw, NC), (mlp_nw, s_mlpw, NC),
                        (ln_g, s_lng, NC), (ln_b, s_lnb, NC), (gnorm_w, s_gnw, 2)]:
        for m in range(n):
            OP("sync", lambda e, s=src, d=dst, m=m: e.dma_start(
                out=d[:, m:m + 1], in_=s[0:1, 128 * m:128 * (m + 1)].rearrange("o p -> p o")),
               inc="wc", inc_n=16)
    OP("sync", lambda e: e.dma_start(out=s_row[0:1, 0:H * W], in_=c_qsc), inc="wc", inc_n=16)
    OP("sync", lambda e: e.dma_start(out=s_m1[0:64, :], in_=spl_b), inc="wc", inc_n=16)
    OP("sync", lambda e: e.dma_start(out=s_m2[0:64, :], in_=spl_s), inc="wc", inc_n=16)
    wconst = N("wc")
    wqrow = wspl = wconst
    OP("vector", lambda e: e.memset(s_halo[:, :], 0.0), inc="dv")
    OP("vector", lambda e: e.memset(s_tmm[:, :], 0.0), inc="dv")

    for h in range(H):
        OP("tensor", lambda e, h=h: e.matmul(
            pS[:, 0:128], s_ones1[0:1, :], s_row[0:1, h * W:(h + 1) * W],
            start=True, stop=True),
           waits=[("wc", wconst), ("dv", N("dv"))], inc="pe")
        OP("vector", lambda e, h=h: e.tensor_copy(s_qsc[:, h * W:(h + 1) * W], pS[:, 0:128]),
           waits=[("pe", N("pe"))], inc="dv")
    dv_qsc = N("dv")

    OP("vector", lambda e: e.tensor_reduce(
        s_scr2[0:64, 0:DSL].rearrange("p (a o) -> p a o", o=1),
        s_m2[0:64, :].rearrange("p (j g) -> p j g", g=GRID), AX.X, ALU.add),
       waits=[("wc", wspl)], inc="dv")
    OP("scalar", lambda e: e.activation(s_scr2[0:64, 256:256 + DSL], s_scr2[0:64, 0:DSL],
                                        AF.Sign),
       waits=[("dv", N("dv"))], inc="ac")
    OP("gpsimd", lambda e: e.dma_start(
        out=ss_dram[0:1, :].rearrange("o (t j) -> (o t) j", j=DSL),
        in_=s_scr2[0:64, 256:256 + DSL]),
       waits=[("ac", N("ac")), ("gd", N("gd"))], inc="gd", inc_n=16)
    for m in range(NC):
        OP("gpsimd", lambda e, m=m: e.dma_start(
            out=s_ssf[:, m:m + 1],
            in_=ss_dram[0:1, 128 * m:128 * (m + 1)].rearrange("o p -> p o")),
           waits=[("gd", N("gd"))], inc="gd", inc_n=16)
    gd_ssf = N("gd")
    OP("vector", lambda e: e.tensor_tensor(s_gssf[:, :], s_lng[:, :], s_ssf[:, :],
                                           ALU.mult),
       waits=[("gd", gd_ssf), ("wc", wconst)], inc="dv")
    OP("vector", lambda e: e.tensor_tensor(s_bssf[:, :], s_lnb[:, :], s_ssf[:, :],
                                           ALU.mult), inc="dv")
    dv_ssfp = N("dv")
    OP("tensor", lambda e: e.matmul(pS[0:64, 0:512], s_ones1[0:1, 0:64], s_grow[0:1, :],
                                    start=True, stop=True),
       waits=[("wc", wconst), ("dv", N("dv"))], inc="pe")
    OP("vector", lambda e: e.tensor_tensor(s_scr2[0:64, 0:512], s_m2[0:64, :], pS[0:64, 0:512],
                                           ALU.mult),
       waits=[("pe", N("pe")), ("ac", N("ac")), ("gd", N("gd"))], inc="dv")
    OP("vector", lambda e: e.tensor_tensor(s_m1[0:64, :], s_m1[0:64, :], s_scr2[0:64, 0:512],
                                           ALU.subtract), inc="dv")
    dv_m1 = N("dv")
    OP("vector", lambda e: e.tensor_copy(
        s_m1g[0:64, :].rearrange("p (gg j) -> p gg j", j=DSL),
        s_m1[0:64, :].rearrange("p (j gg) -> p gg j", gg=GRID)),
       waits=[("dv", dv_m1)], inc="dv")
    OP("vector", lambda e: e.tensor_copy(
        s_m2g[0:64, :].rearrange("p (gg j) -> p gg j", j=DSL),
        s_m2[0:64, :].rearrange("p (j gg) -> p gg j", gg=GRID)),
       waits=[("wc", wspl)], inc="dv")
    dv_mg = N("dv")
    OP("gpsimd", lambda e: e.dma_start(out=s_m1g[64:128, :], in_=s_m1g[0:64, :]),
       waits=[("dv", dv_mg), ("gd", N("gd"))], inc="gd", inc_n=16)
    OP("gpsimd", lambda e: e.dma_start(out=s_m2g[64:128, :], in_=s_m2g[0:64, :]),
       waits=[("dv", dv_mg), ("gd", N("gd"))], inc="gd", inc_n=16)
    gd_mdup = N("gd")

    stg = s_t32[:, :].bitcast(BF16)
    cast_jobs = []
    order = [("wq", v_w["wq"], w_b["wq"], D, D), ("wk", v_w["wk"], w_b["wk"], D, D),
             ("wv", v_w["wv"], w_b["wv"], D, D),
             ("diags", v_diags, diags_b, 3 * NC * 4 * 128, 128),
             ("wg", v_w["wg"], w_b["wg"], D, D), ("wo", v_w["wo"], w_b["wo"], D, D),
             ("wgate", v_wgate, wgate_b, D, 2 * INTER),
             ("wdown", v_wdown, wdown_b, INTER, D)]
    for (nm, src, dst, rows, cols) in order:
        if cols <= 2816:
            for r in range(rows // 128):
                cast_jobs.append((nm, src[128 * r:128 * (r + 1), :],
                                  dst[128 * r:128 * (r + 1), :], cols))
        else:
            half = cols // 2
            for r in range(rows // 128):
                for cp in range(2):
                    cast_jobs.append((nm, src[128 * r:128 * (r + 1), cp * half:(cp + 1) * half],
                                      dst[128 * r:128 * (r + 1), cp * half:(cp + 1) * half],
                                      half))
    slot_ap = [lambda cc: stg[:, 0:cc], lambda cc: stg[:, INTER:INTER + cc]]
    _cast_state = {"i": 0}

    def emit_cast_steps(njobs):
        for _ in range(njobs):
            ci = _cast_state["i"]
            if ci >= len(cast_jobs):
                return
            _cast_state["i"] = ci + 1
            (nm, sap, dap, ccols) = cast_jobs[ci]
            sn = ci % 2
            cisem, cosem = f"ci{sn}", f"co{sn}"
            ap = slot_ap[sn](ccols)
            OP("gpsimd", lambda e, s=sap, ap=ap: e.dma_start(out=ap, in_=s),
               waits=[(cosem, N(cosem)), ("gd", gd_mdup)], inc=cisem, inc_n=16)
            OP("sync", lambda e, d=dap, ap=ap: e.dma_start(out=d, in_=ap),
               waits=[(cisem, N(cisem))], inc=cosem, inc_n=16)

    emit_cast_steps(8)

    if "m1t" in taps:
        tapout("m1t", tap_t["m1t"][:, :], s_m1[:, :], [("gd", N("gd")), ("dv", N("dv"))])
        tapout("m2t", tap_t["m2t"][:, :], s_m2[:, :], [("gd", N("gd")), ("dv", N("dv"))])
    dv_tile_free = N("dv")
    pe_pp23_free = 0
    for ti in range(NTILE):
        t0 = ti * TT
        prow = 64 * (ti % 2)
        xrow = 32 * (ti % 2)
        gcol0 = (ti // 2) * TT
        oht_tile = s_oht[prow:prow + 64, gcol0:gcol0 + TT]
        ohtb_tile = s_ohtb[prow:prow + 64, gcol0:gcol0 + TT]
        for m in range(NC):
            OP("sync", lambda e, m=m, t0=t0: e.dma_start(
                out=XNc(m), in_=x_fm[128 * m:128 * (m + 1), t0:t0 + TT]),
               waits=[("dv", dv_tile_free), ("wc", wconst)], inc="wxt", inc_n=16)
        xtile = N("wxt")
        emit_cast_steps(10)
        for m in range(NC):
            OP("scalar", lambda e, m=m: e.activation(
                s_scr[:, (m % 2) * TT:(m % 2) * TT + TT], XNc(m), AF.Square),
               waits=[("wxt", xtile), ("pe", N("pe") if m >= 2 else 0)], inc="ac")
            OP("tensor", lambda e, m=m: e.matmul(
                pS[0:1, 0:TT], s_ones128[:, :], XNc(m), start=(m == 0), stop=(m == NC - 1)),
               waits=[("wxt", xtile)], inc="pe")
            OP("tensor", lambda e, m=m: e.matmul(
                pD[0:1, 0:TT], s_ones128[:, :],
                s_scr[:, (m % 2) * TT:(m % 2) * TT + TT],
                start=(m == 0), stop=(m == NC - 1)),
               waits=[("ac", N("ac"))], inc="pe")
        pe_sums = N("pe")
        dvw = [("pe", pe_sums)]
        OP("vector", lambda e: e.tensor_scalar(R(5), pS[0:1, 0:TT], 1.0 / D, None,
                                               ALU.mult, ALU.bypass), waits=dvw, inc="dv")
        OP("vector", lambda e: e.tensor_scalar(R(9), pD[0:1, 0:TT], 1.0 / D, None,
                                               ALU.mult, ALU.bypass), inc="dv")
        OP("vector", lambda e: e.tensor_tensor(R(6), R(5), R(5), ALU.mult), inc="dv")
        OP("vector", lambda e: e.tensor_tensor(R(6), R(9), R(6), ALU.subtract), inc="dv")
        OP("vector", lambda e: e.tensor_scalar(R(6), R(6), EPS, None, ALU.add,
                                               ALU.bypass), waits=[("dv", N("dv"))],
           inc="dv")
        OP("scalar", lambda e: e.activation(R(0), R(6), AF.Sqrt),
           waits=[("dv", N("dv"))], inc="ac")
        OP("vector", lambda e: e.reciprocal(R(0), R(0)), waits=[("ac", N("ac"))], inc="dv")
        OP("vector", lambda e: e.tensor_tensor(R(9), R(0), R(0), ALU.mult), inc="dv")
        OP("vector", lambda e: e.tensor_tensor(R(9), R(6), R(9), ALU.mult), inc="dv")
        OP("vector", lambda e: e.tensor_scalar(R(9), R(9), -0.5, 1.5, ALU.mult, ALU.add),
           inc="dv")
        OP("vector", lambda e: e.tensor_tensor(R(0), R(0), R(9), ALU.mult), inc="dv")
        OP("vector", lambda e: e.tensor_tensor(R(1), R(0), R(0), ALU.mult), inc="dv")
        OP("vector", lambda e: e.tensor_tensor(R(1), R(6), R(1), ALU.mult), inc="dv")
        OP("vector", lambda e: e.tensor_scalar(R(9), pS[0:1, 0:TT], 1.0 / D, None,
                                               ALU.mult, ALU.bypass), inc="dv")
        OP("vector", lambda e: e.tensor_tensor(R(4), R(9), R(5), ALU.subtract), inc="dv")
        OP("vector", lambda e: e.tensor_tensor(R(4), R(4), R(0), ALU.mult), inc="dv")
        OP("vector", lambda e: e.tensor_tensor(R(9), R(4), R(4), ALU.mult), inc="dv")
        OP("vector", lambda e: e.tensor_tensor(R(9), R(1), R(9), ALU.subtract), inc="dv")
        OP("scalar", lambda e: e.activation(R(3), R(9), AF.Sqrt, bias=EPS),
           waits=[("dv", N("dv"))], inc="ac")
        OP("vector", lambda e: e.reciprocal(R(3), R(3)), waits=[("ac", N("ac"))], inc="dv")
        OP("scalar", lambda e: e.activation(R(1), R(1), AF.Sqrt, bias=EPS),
           waits=[("dv", N("dv"))], inc="ac")
        OP("vector", lambda e: e.reciprocal(R(1), R(1)), waits=[("ac", N("ac"))], inc="dv")
        OP("vector", lambda e: e.tensor_tensor(R(8), R(0), R(3), ALU.mult), inc="dv")
        OP("vector", lambda e: e.tensor_tensor(R(2), R(5), R(0), ALU.mult), inc="dv")
        OP("vector", lambda e: e.tensor_scalar(R(2), R(2), -1.0, None, ALU.mult,
                                               ALU.bypass), inc="dv")
        OP("vector", lambda e: e.tensor_tensor(R(7), R(4), R(2), ALU.subtract), inc="dv")
        OP("vector", lambda e: e.tensor_scalar(R(7), R(7), -1.0, None, ALU.mult,
                                               ALU.bypass), inc="dv")
        OP("vector", lambda e: e.tensor_tensor(R(7), R(7), R(3), ALU.mult), inc="dv")
        dv_rows = N("dv")
        OP("gpsimd", lambda e, ti=ti: e.dma_start(
            out=stat_dram[ti:ti + 1, :], in_=s_row[0:1, 0:5 * TT]),
           waits=[("dv", dv_rows), ("gd", N("gd"))], inc="gd", inc_n=16)
        OP("tensor", lambda e: e.matmul(pp[0][:, 0:TT], s_ones1[0:1, :], R(8),
                                        start=True, stop=True),
           waits=[("dv", dv_rows)], inc="pe")
        OP("tensor", lambda e: e.matmul(pp[1][:, 0:TT], s_ones1[0:1, :], R(7),
                                        start=True, stop=True), inc="pe")
        pe_bc = N("pe")
        for m in range(NC):
            OP("vector", lambda e, m=m: e.tensor_tensor(XFc(m), XNc(m), pp[0][:, 0:TT],
                                                        ALU.mult),
               waits=[("pe", pe_bc)] + ([("gd", gd_ssf)] if (ti == 0 and m == 0) else []),
               inc="dv")
            OP("vector", lambda e, m=m: e.tensor_tensor(
                XFc(m), XFc(m), pp[1][:, 0:TT], ALU.add), inc="dv")
        dv_xf = N("dv")
        if "xf" in taps:
            for m in range(NC):
                tapout("xf", tap_t["xf"][128 * m:128 * (m + 1), t0:t0 + TT], XFc(m),
                       [("dv", dv_xf)])
        for m in range(NC):
            OP("scalar", lambda e, m=m: e.activation(
                s_as[:, 0:TT].bitcast(BF16)[:, 0:TT], XFc(m), AF.Sign,
                scale=s_gssf[:, m:m + 1], bias=s_bssf[:, m:m + 1]),
               waits=[("dv", max(dv_xf, dv_ssfp)),
                      ("pe", N("pe") if m >= 1 else 0)],
               inc="ac")
            OP("tensor", lambda e, m=m: e.matmul(
                pS[0:64, 0:TT], s_selsc[:, 64 * m:64 * (m + 1)], s_as[:, 0:TT].bitcast(BF16)[:, 0:TT],
                start=(m == 0), stop=(m == NC - 1)),
               waits=[("ac", N("ac"))], inc="pe")
        pe_scores = N("pe")
        emit_cast_steps(10)
        OP("vector", lambda e: e.tensor_copy(s_scr[0:64, 0:TT], pS[0:64, 0:TT]),
           waits=[("pe", pe_scores)], inc="dv")
        dv_scev = N("dv")
        if "scores" in taps:
            tapout("scores", tap_t["scores"][0:64, t0:t0 + TT], s_scr[0:64, 0:TT],
                   [("dv", dv_scev)])
        for c in range(STT):
            OP("tensor", lambda e, c=c: e.transpose(
                pp[2][:, c * 64:(c + 1) * 64], s_scr[0:64, c * 128:(c + 1) * 128],
                s_id64[0:64, 0:64]),
               waits=[("dv", dv_scev), ("pe", pe_pp23_free)], inc="pe")
        pe_sct = N("pe")
        OP("vector", lambda e: e.tensor_copy(s_sc_tm[:, 0:STT * 64], pp[2][:, 0:STT * 64]),
           waits=[("pe", pe_sct)], inc="dv")
        if "sctm" in taps:
            tapout("sctm", tap_t["sctm"][:, ti * STT * 64:(ti + 1) * STT * 64],
                   s_sc_tm[:, 0:STT * 64], [("dv", N("dv"))])
        for c in range(STT):
            SC = s_sc_tm[:, c * 64:(c + 1) * 64]
            OP("vector", lambda e, SC=SC: e.tensor_reduce(
                s_rb[:, 0:8].rearrange("p (a o) -> p a o", o=1),
                SC.rearrange("p (a b) -> p a b", b=8), AX.X, ALU.add), inc="dv")
            OP("vector", lambda e: e.reduce_max(s_tmm[:, 0:1], s_rb[:, 0:8], AX.X), inc="dv")
            OP("vector", lambda e: e.tensor_scalar(
                s_scr2[:, 0:8], s_rb[:, 0:8], s_tmm[:, 0:1], None, ALU.is_equal,
                ALU.bypass), waits=[("pe", N("pe"))], inc="dv")
            OP("vector", lambda e: e.tensor_tensor(
                s_scr2[:, 0:8], s_scr2[:, 0:8], s_riota8[:, :], ALU.mult), inc="dv")
            OP("vector", lambda e: e.reduce_max(s_tmm[:, 1:2], s_scr2[:, 0:8], AX.X),
               inc="dv")
            OP("vector", lambda e: e.tensor_scalar(
                s_tmm[:, 1:2], s_tmm[:, 1:2], -1.0, 7.0, ALU.mult, ALU.add), inc="dv")
            OP("vector", lambda e: e.tensor_scalar(
                s_scr2[:, 64:128], s_iotac64[:, :], s_tmm[:, 1:2], None, ALU.is_equal,
                ALU.bypass), inc="dv")
            OP("vector", lambda e, SC=SC: e.scalar_tensor_tensor(
                s_scr2[:, 64:128], s_scr2[:, 64:128], 512.0, SC, ALU.mult, ALU.add),
               inc="dv")
            OP("vector", lambda e: e.reduce_max(s_tmm[:, 2:3], s_scr2[:, 64:128], AX.X),
               inc="dv")
            OP("vector", lambda e: e.tensor_scalar(
                s_scr2[:, 0:64], s_scr2[:, 64:128], s_tmm[:, 2:3], None, ALU.is_equal,
                ALU.bypass), inc="dv")
            OP("vector", lambda e: e.tensor_tensor(
                s_scr2[:, 0:64], s_scr2[:, 0:64], s_riota64[:, :], ALU.mult), inc="dv")
            OP("vector", lambda e: e.reduce_max(s_tmm[:, 3:4], s_scr2[:, 0:64], AX.X),
               inc="dv")
            OP("vector", lambda e: e.tensor_scalar(
                s_tmm[:, 3:4], s_tmm[:, 3:4], -1.0, 63.0, ALU.mult, ALU.add), inc="dv")
            OP("vector", lambda e: e.tensor_scalar(
                s_scr2[:, 0:64], s_iota64[:, :], s_tmm[:, 3:4], None, ALU.is_equal,
                ALU.bypass), inc="dv")
            dv_oh = N("dv")
            if "amx" in taps:
                gc_ = ti * STT + c
                tapout("amx", tap_t["amx"][:, gc_ * 12:gc_ * 12 + 8], s_rb[:, 0:8],
                       [("dv", dv_oh)])
                tapout("amx", tap_t["amx"][:, gc_ * 12 + 8:gc_ * 12 + 12], s_tmm[:, 0:4],
                       [("dv", dv_oh)])
            OP("tensor", lambda e: e.transpose(
                pp[3][0:64, 0:128], s_scr2[:, 0:64], s_identf[:, :]),
               waits=[("dv", dv_oh)], inc="pe")
            pe_oh = N("pe")
            pcol = gcol0 + c * 128
            OP("vector", lambda e, pcol=pcol, prow=prow: e.tensor_copy(
                s_oht[prow:prow + 64, pcol:pcol + 128], pp[3][0:64, 0:128]),
               waits=[("pe", pe_oh)], inc="dv")
            OP("vector", lambda e, pcol=pcol, prow=prow: e.tensor_copy(
                s_ohtb[prow:prow + 64, pcol:pcol + 128], pp[3][0:64, 0:128]), inc="dv")
        dv_oht = N("dv")
        pe_pp23_free = N("pe")
        for m in range(NC):
            OP("tensor", lambda e, m=m, prow=prow, ot=ohtb_tile: e.matmul(
                pp[m % 2][:, 0:TT], s_ek2[prow:prow + 64, 128 * m:128 * (m + 1)], ot,
                start=True, stop=True),
               waits=[("dv", dv_oht if m < 2 else N("dv"))], inc="pe")
            OP("vector", lambda e, m=m: e.tensor_scalar(
                s_scr2[:, (m % 2) * TT:(m % 2) * TT + TT], XFc(m), s_lng[:, m:m + 1],
                s_lnb[:, m:m + 1], ALU.mult, ALU.add), inc="dv")
            OP("vector", lambda e, m=m: e.tensor_tensor(
                s_scr[:, (m % 2) * TT:(m % 2) * TT + TT],
                s_scr2[:, (m % 2) * TT:(m % 2) * TT + TT], pp[m % 2][:, 0:TT],
                ALU.mult),
               waits=[("pe", N("pe"))], inc="dv")
            OP("tensor", lambda e, m=m: e.matmul(
                pS[0:32, 0:TT], s_rfold[:, :], s_scr[:, (m % 2) * TT:(m % 2) * TT + TT],
                start=(m == 0), stop=(m == NC - 1)),
               waits=[("dv", N("dv"))], inc="pe")
        pe_xsl = N("pe")
        OP("vector", lambda e, xrow=xrow, gcol0=gcol0: e.tensor_copy(
            s_xsl[xrow:xrow + 32, gcol0:gcol0 + TT], pS[0:32, 0:TT]),
           waits=[("pe", pe_xsl)], inc="dv")
        OP("vector", lambda e: e.tensor_copy(s_scr2[0:32, 0:TT], pS[0:32, 0:TT]), inc="dv")
        dv_xslev = N("dv")
        for c in range(STT):
            OP("tensor", lambda e, c=c: e.transpose(
                pp[2][:, 0:32], s_scr2[0:32, c * 128:(c + 1) * 128],
                s_id64[0:32, 0:32]),
               waits=[("dv", dv_xslev if c == 0 else N("dv"))], inc="pe")
            pe_tkt = N("pe")
            OP("vector", lambda e: e.tensor_reduce(
                s_tmm[:, 4:5], pp[2][:, 0:32], AX.X, ALU.min),
               waits=[("pe", pe_tkt)], inc="dv")
            OP("vector", lambda e: e.tensor_reduce(
                s_tmm[:, 5:6], pp[2][:, 0:32], AX.X, ALU.max), inc="dv")
            pcol = gcol0 + c * 128
            OP("tensor", lambda e, pcol=pcol, prow=prow: e.transpose(
                pp[3][:, 0:64], s_oht[prow:prow + 64, pcol:pcol + 128],
                s_id64[prow:prow + 64, 0:64]), waits=[("dv", N("dv"))], inc="pe")
            pe_ohb = N("pe")
            OP("vector", lambda e: e.tensor_scalar(
                s_scr2[:, 320:384], pp[3][:, 0:64], BIG, None, ALU.mult, ALU.bypass),
               waits=[("pe", pe_ohb)], inc="dv")
            OP("vector", lambda e: e.tensor_scalar(
                s_scr2[:, 320:384], s_scr2[:, 320:384], -1.0, BIG, ALU.mult, ALU.add),
               inc="dv")
            OP("vector", lambda e: e.scalar_tensor_tensor(
                s_scr2[:, 256:320], pp[3][:, 0:64], s_tmm[:, 4:5], s_scr2[:, 320:384],
                ALU.mult, ALU.add), inc="dv")
            dv_t1 = N("dv")
            OP("tensor", lambda e: e.transpose(
                pp[3][0:64, 128:256], s_scr2[:, 256:320], s_identf[:, :]),
               waits=[("dv", dv_t1)], inc="pe")
            OP("vector", lambda e: e.tensor_reduce(
                s_scr2[0:64, 384:385], pp[3][0:64, 128:256], AX.X, ALU.min),
               waits=[("pe", N("pe"))], inc="dv")
            if ti == 0 and c == 0:
                OP("vector", lambda e: e.tensor_copy(s_tmm[0:64, 7:8],
                                                     s_scr2[0:64, 384:385]), inc="dv")
            else:
                OP("vector", lambda e: e.tensor_tensor(
                    s_tmm[0:64, 7:8], s_tmm[0:64, 7:8], s_scr2[0:64, 384:385], ALU.min),
                   inc="dv")
            OP("vector", lambda e: e.tensor_scalar(
                s_scr2[:, 320:384], pp[3][:, 0:64], BIG, -BIG, ALU.mult, ALU.add),
               waits=[("pe", N("pe"))], inc="dv")
            OP("vector", lambda e: e.scalar_tensor_tensor(
                s_scr2[:, 256:320], pp[3][:, 0:64], s_tmm[:, 5:6], s_scr2[:, 320:384],
                ALU.mult, ALU.add), inc="dv")
            dv_t2 = N("dv")
            OP("tensor", lambda e: e.transpose(
                pp[3][0:64, 128:256], s_scr2[:, 256:320], s_identf[:, :]),
               waits=[("dv", dv_t2)], inc="pe")
            OP("vector", lambda e: e.tensor_reduce(
                s_scr2[0:64, 384:385], pp[3][0:64, 128:256], AX.X, ALU.max),
               waits=[("pe", N("pe"))], inc="dv")
            if ti == 0 and c == 0:
                OP("vector", lambda e: e.tensor_copy(s_tmm[0:64, 6:7],
                                                     s_scr2[0:64, 384:385]), inc="dv")
            else:
                OP("vector", lambda e: e.tensor_tensor(
                    s_tmm[0:64, 6:7], s_tmm[0:64, 6:7], s_scr2[0:64, 384:385], ALU.max),
                   inc="dv")
        dv_tile_free = N("dv")
        pe_pp23_free = N("pe")

    dv_sweep1 = N("dv")
    if "onehot" in taps:
        tapout("onehot", tap_t["onehot"][:, :], s_oht[:, :], [("dv", dv_sweep1)])
    if "xsl" in taps:
        tapout("xsl", tap_t["xsl"][0:64, :], s_xsl[:, :], [("dv", dv_sweep1)])
    if "stats" in taps:
        tapout("stats", tap_t["stats"][:, :], stat_dram, [("dv", dv_sweep1)])

    emit_cast_steps(10 ** 9)
    cast_done = [c for c in [("co0", N("co0")), ("co1", N("co1"))] if c[1] > 0]

    if stage >= 2:
        OP("vector", lambda e: e.memset(s_cc[:, :], -BIG), waits=[("dv", dv_sweep1)],
           inc="dv")
        OP("vector", lambda e: e.tensor_copy(s_cc[0:64, 0:1], s_tmm[0:64, 6:7]), inc="dv")
        OP("vector", lambda e: e.tensor_scalar(
            s_cc[64:128, 0:1], s_tmm[0:64, 7:8], -1.0, None, ALU.mult, ALU.bypass),
           inc="dv")
        dv_cc = N("dv")
        OP("gpsimd", lambda e: e.dma_start(out=cc_in[:, :], in_=s_cc[:, :]),
           waits=[("dv", dv_cc), ("gd", N("gd"))], inc="gd", inc_n=16)
        OP("gpsimd", lambda e: e.collective_compute(
            "AllReduce", ALU.max, replica_groups=[list(range(NCORE))],
            ins=[cc_in.ap().opt()], outs=[cc_out.ap().opt()]),
           waits=[("gd", N("gd"))], inc="cc")
        OP("gpsimd", lambda e: e.dma_start(out=s_cc[:, :], in_=cc_out[:, :]),
           waits=[("cc", N("cc"))], inc="gd", inc_n=16)
        gd_cc = N("gd")
        OP("vector", lambda e: e.tensor_copy(s_gmm[0:64, 0:1], s_cc[0:64, 0:1]),
           waits=[("gd", gd_cc)], inc="dv")
        OP("vector", lambda e: e.tensor_scalar(
            s_gmm[0:64, 1:2], s_cc[64:128, 0:1], -1.0, None, ALU.mult, ALU.bypass),
           inc="dv")
        OP("vector", lambda e: e.tensor_tensor(
            s_gmm[0:64, 2:3], s_gmm[0:64, 0:1], s_gmm[0:64, 1:2], ALU.subtract), inc="dv")
        OP("vector", lambda e: e.tensor_scalar(
            s_gmm[0:64, 2:3], s_gmm[0:64, 2:3], 1e-8, None, ALU.add, ALU.bypass), inc="dv")
        OP("vector", lambda e: e.reciprocal(s_gmm[0:64, 2:3], s_gmm[0:64, 2:3]), inc="dv")
        dv_gmm = N("dv")
        OP("gpsimd", lambda e: e.dma_start(out=s_gmm[64:128, 0:3], in_=s_gmm[0:64, 0:3]),
           waits=[("dv", dv_gmm), ("gd", N("gd"))], inc="gd", inc_n=16)
        gd_gmm = N("gd")
        if "gmm" in taps:
            tapout("gmm", tap_t["gmm"][:, :], s_gmm[:, :], [("gd", gd_gmm)])
    else:
        gd_gmm = N("gd")

    if stage >= 3:
        OP("vector", lambda e: e.memset(s_t32[:, :], 0.0),
           waits=cast_done, inc="dv")
        OP("vector", lambda e: e.memset(s_tbf[:, :], 0.0), inc="dv")
        dv_tinit = N("dv")

        bank_free = {0: ("dv", 0), 1: ("dv", 0), 2: ("dv", 0), 3: ("dv", 0)}
        tb_cast = {}
        wslot_pe = [0, 0, 0, 0]
        wslot_wd = [0, 0, 0, 0]
        dslot_pe = [0, 0, 0, 0]
        dslot_wd = [0, 0, 0, 0]
        xslot_dv = [0, 0]
        xslot_wd = [0, 0]
        oslot_od = [0, 0]
        oslot_dv = [0, 0]

        def stream_w(wdram, kchunks, mi, slot):
            sl = wdram[:, 128 * mi:128 * (mi + 1)].rearrange("(kc p) m -> p kc m", p=128)
            sem = f"ww{slot}"
            OP("sync", lambda e, s=sl, slot=slot, n=kchunks * 128: e.dma_start(
                out=s_wb[:, slot * 2048:slot * 2048 + n].rearrange(
                    "p (kc m) -> p kc m", m=128), in_=s),
               waits=[("pe", wslot_pe[slot])] + cast_done, inc=sem, inc_n=16)
            wslot_wd[slot] = N(sem)

        def sync_wslots():
            mx = max(wslot_pe)
            for i_ in range(4):
                wslot_pe[i_] = mx

        for ti in range(NTILE):
            t0 = ti * TT
            prow = 64 * (ti % 2)
            xrow = 32 * (ti % 2)
            gcol0 = (ti // 2) * TT
            oht_tile = s_oht[prow:prow + 64, gcol0:gcol0 + TT]
            ohtb_tile = s_ohtb[prow:prow + 64, gcol0:gcol0 + TT]

            RB = 5 * (ti % 2)
            if ti == 0:
                for (ss_, dd_) in [(0, 0), (2, 2), (1, 1)]:
                    OP("gpsimd", lambda e, ss_=ss_, dd_=dd_: e.dma_start(
                        out=R(dd_), in_=stat_dram[0:1, ss_ * TT:(ss_ + 1) * TT]),
                       waits=[("gd", N("gd"))], inc="wr", inc_n=16)
            wd_rows = N("wr")
            OP("tensor", lambda e, RB=RB: e.matmul(
                pp[0][:, 0:TT], s_ones1[0:1, :], R(RB + 0), start=True, stop=True),
               waits=[("wr", wd_rows), bank_free[0], bank_free[1], bank_free[2],
                      ("ac", N("ac"))], inc="pe")
            OP("tensor", lambda e, RB=RB: e.matmul(
                pp[1][:, 0:TT], s_ones1[0:1, :], R(RB + 2), start=True, stop=True),
               inc="pe")
            OP("tensor", lambda e, RB=RB: e.matmul(
                pp[2][:, 0:TT], s_ones1[0:1, :], R(RB + 1), start=True, stop=True),
               inc="pe")
            pe_rows = N("pe")
            if ti + 1 < NTILE:
                RB2 = 5 * ((ti + 1) % 2)
                for (ss_, dd_) in [(0, 0), (2, 2), (1, 1)]:
                    OP("gpsimd", lambda e, ss_=ss_, dd_=dd_, ti=ti, RB2=RB2: e.dma_start(
                        out=R(RB2 + dd_),
                        in_=stat_dram[ti + 1:ti + 2, ss_ * TT:(ss_ + 1) * TT]),
                       waits=[("pe", N("pe")), ("gd", N("gd"))], inc="wr", inc_n=16)
            xw = []
            for m in range(NC):
                sl = m % 2
                OP("gpsimd", lambda e, m=m, sl=sl, t0=t0: e.dma_start(
                    out=s_xc[:, sl * TT:(sl + 1) * TT],
                    in_=x_fm[128 * m:128 * (m + 1), t0:t0 + TT]),
                   waits=[("dv", xslot_dv[sl])], inc=f"wx{sl}", inc_n=16)
                xslot_wd[sl] = N(f"wx{sl}")
                OP("vector", lambda e, m=m, sl=sl: e.tensor_tensor(
                    XNc(m), s_xc[:, sl * TT:(sl + 1) * TT], pp[0][:, 0:TT], ALU.mult),
                   waits=[(f"wx{sl}", xslot_wd[sl]), ("pe", pe_rows)], inc="dv")
                OP("vector", lambda e, m=m: e.tensor_tensor(
                    XNc(m), XNc(m), pp[1][:, 0:TT], ALU.add), inc="dv")
                xslot_dv[sl] = N("dv")
                OP("vector", lambda e, m=m: e.tensor_tensor(
                    s_scr2[:, 0:TT], XNc(m), pp[2][:, 0:TT], ALU.mult), inc="dv")
                OP("vector", lambda e, m=m: e.tensor_scalar(
                    ATCH(m), s_scr2[:, 0:TT], s_attnw[:, m:m + 1], None, ALU.mult,
                    ALU.bypass), inc="dv")
            dv_xn = N("dv")
            if "xn" in taps:
                for m in range(NC):
                    tapout("xn", tap_t["xn"][128 * m:128 * (m + 1), t0:t0 + TT], XNc(m),
                           [("dv", dv_xn)])
            if "at" in taps:
                for m in range(NC):
                    tapout("at", tap_t["at"][128 * m:128 * (m + 1), t0:t0 + TT], ATCH(m),
                           [("dv", dv_xn)])

            if stage < 5:
                continue

            def proj(widx, wdram, diag_base, out_kind):
                for m in range(NC):
                    sl = m % 4
                    stream_w(wdram, NC, m, sl)
                    for k in range(NC):
                        OP("tensor", lambda e, m=m, k=k, sl=sl: e.matmul(
                            pp[m % 2][:, 0:TT],
                            s_wb[:, sl * 2048 + 128 * k:sl * 2048 + 128 * (k + 1)],
                            ATCH(k), start=(k == 0), stop=(k == NC - 1)),
                           waits=([(f"ww{sl}", wslot_wd[sl]), ("dv", dv_xn),
                                   bank_free[m % 2]] if k == 0 else []),
                           inc="pe")
                    wslot_pe[sl] = N("pe")
                    pe_proj = N("pe")
                    if out_kind == "g":
                        OP("scalar", lambda e, m=m: e.activation(
                            SGCH(m), pp[m % 2][:, 0:TT], AF.Silu),
                           waits=[("pe", pe_proj)], inc="ac")
                        bank_free[sl] = ("ac", N("ac"))
                        continue
                    qsl = m % 2
                    base = qsl * (TT + 4)
                    OP("vector", lambda e, m=m, base=base: e.tensor_copy(
                        s_q0e[:, base + 3:base + 3 + TT], pp[m % 2][:, 0:TT]),
                       waits=[("pe", pe_proj), ("pe", dslot_pe[qsl])], inc="dv")
                    bank_free[sl] = ("dv", N("dv"))
                    hb = (widx * NC + m) * 3
                    OP("vector", lambda e, hb=hb, base=base: e.tensor_copy(
                        s_q0e[:, base:base + 3], s_halo[:, hb:hb + 3]), inc="dv")
                    OP("vector", lambda e, hb=hb, base=base: e.tensor_copy(
                        s_halo[:, hb:hb + 3], s_q0e[:, base + TT:base + TT + 3]), inc="dv")
                    dv_q0 = N("dv")
                    dsl = m % 4
                    drow = (diag_base + m * 4) * 128
                    OP("sync", lambda e, drow=drow, dsl=dsl: e.dma_start(
                        out=s_kt[:, dsl * 512:dsl * 512 + 512].rearrange(
                            "p (i c) -> p i c", c=128),
                        in_=diags_b[drow:drow + 512, :].rearrange(
                            "(i p) c -> p i c", p=128)),
                       waits=[("pe", dslot_pe[dsl])] + cast_done,
                       inc=f"wg{dsl}", inc_n=16)
                    dslot_wd[dsl] = N(f"wg{dsl}")
                    cbk = 2 + (m % 2)
                    for i in range(4):
                        OP("tensor", lambda e, i=i, dsl=dsl, base=base, cbk=cbk: e.matmul(
                            pp[cbk][:, 0:TT],
                            s_kt[:, dsl * 512 + 128 * i:dsl * 512 + 128 * (i + 1)],
                            s_q0e[:, base + i:base + i + TT],
                            start=(i == 0), stop=(i == 3)),
                           waits=([(f"wg{dsl}", dslot_wd[dsl]), ("dv", dv_q0),
                                   bank_free[cbk]] if i == 0 else []),
                           inc="pe")
                    dslot_pe[dsl] = N("pe")
                    pe_conv = N("pe")
                    if out_kind == "q":
                        OP("scalar", lambda e, m=m, cbk=cbk: e.activation(
                            QF(m), pp[cbk][:, 0:TT], AF.Silu),
                           waits=[("pe", pe_conv)], inc="ac")
                        bank_free[cbk] = ("ac", N("ac"))
                    elif out_kind == "k":
                        OP("scalar", lambda e, m=m, cbk=cbk: e.activation(
                            KF(m), pp[cbk][:, 0:TT], AF.Silu),
                           waits=[("pe", pe_conv)], inc="ac")
                        bank_free[cbk] = ("ac", N("ac"))
                    else:
                        OP("scalar", lambda e, m=m, cbk=cbk: e.activation(
                            s_as[:, 0:TT].bitcast(BF16)[:, 0:TT], pp[cbk][:, 0:TT],
                            AF.Silu),
                           waits=[("pe", pe_conv)], inc="ac")
                        bank_free[cbk] = ("ac", N("ac"))
                        ac_v = N("ac")
                        for c in range(STT):
                            OP("tensor", lambda e, c=c: e.transpose(
                                pDb[:, 0:128], s_as[:, 0:TT].bitcast(BF16)[:, c * 128:(c + 1) * 128],
                                s_identb[:, :]),
                               waits=[("ac", ac_v), ("dv", N("dv"))], inc="pe")
                            OP("vector", lambda e, c=c, m=m: e.tensor_copy(
                                VT(c, m * 128, 128), pDb[:, 0:128]),
                               waits=[("pe", N("pe"))], inc="dv")

            proj(0, w_b["wq"], 0 * NC * 4, "q")
            proj(1, w_b["wk"], 1 * NC * 4, "k")
            ac_qk = N("ac")
            cs = s_cos[:, t0:t0 + TT]
            sn = s_sin[:, t0:t0 + TT]
            for h in range(H):
                q1, q2 = QF(2 * h), QF(2 * h + 1)
                k1, k2 = KF(2 * h), KF(2 * h + 1)
                qb = s_qsc[:, h * W:(h + 1) * W].rearrange(
                    "p (o w) -> p o w", o=1).broadcast_to([128, NW, W])
                for (a1, a2, scale) in ((q1, q2, True), (k1, k2, False)):
                    OP("vector", lambda e, a1=a1, cs=cs: e.tensor_tensor(
                        s_scr[:, 0:TT], a1, cs, ALU.mult),
                       waits=[("ac", ac_qk)], inc="dv")
                    OP("vector", lambda e, a2=a2, sn=sn: e.tensor_tensor(
                        s_scr[:, TT:2 * TT], a2, sn, ALU.mult), inc="dv")
                    OP("vector", lambda e, a2=a2, cs=cs: e.tensor_tensor(
                        s_scr2[:, 0:TT], a2, cs, ALU.mult), inc="dv")
                    OP("vector", lambda e, a1=a1, sn=sn: e.tensor_tensor(
                        s_scr2[:, TT:2 * TT], a1, sn, ALU.mult), inc="dv")
                    OP("vector", lambda e, a1=a1: e.tensor_tensor(
                        a1, s_scr[:, 0:TT], s_scr[:, TT:2 * TT], ALU.subtract), inc="dv")
                    OP("vector", lambda e, a2=a2: e.tensor_tensor(
                        a2, s_scr2[:, 0:TT], s_scr2[:, TT:2 * TT], ALU.add), inc="dv")
                    if scale:
                        OP("vector", lambda e, a1=a1, qb=qb: e.tensor_tensor(
                            a1.rearrange("p (o w) -> p o w", w=W), a1.rearrange(
                                "p (o w) -> p o w", w=W), qb, ALU.mult), inc="dv")
                        OP("vector", lambda e, a2=a2, qb=qb: e.tensor_tensor(
                            a2.rearrange("p (o w) -> p o w", w=W), a2.rearrange(
                                "p (o w) -> p o w", w=W), qb, ALU.mult), inc="dv")
            dv_rope = N("dv")
            if "qr" in taps:
                for m in range(NC):
                    OP("vector", lambda e, m=m: e.tensor_copy(s_scr[:, 0:TT], QF(m)),
                       waits=[("dv", dv_rope)], inc="dv")
                    tapout("qr", tap_t["qr"][128 * m:128 * (m + 1), t0:t0 + TT],
                           s_scr[:, 0:TT], [("dv", N("dv"))])
            if "kr" in taps:
                for m in range(NC):
                    OP("vector", lambda e, m=m: e.tensor_copy(s_scr[:, 0:TT], KF(m)),
                       waits=[("dv", dv_rope)], inc="dv")
                    tapout("kr", tap_t["kr"][128 * m:128 * (m + 1), t0:t0 + TT],
                           s_scr[:, 0:TT], [("dv", N("dv"))])

            proj(2, w_b["wv"], 2 * NC * 4, "v")
            proj(3, w_b["wg"], 0, "g")
            dv_vt = N("dv")
            ac_sg = N("ac")
            if "sg" in taps:
                for m in range(NC):
                    OP("vector", lambda e, m=m: e.tensor_copy(s_scr[:, 0:TT], SGCH(m)),
                       waits=[("ac", ac_sg)], inc="dv")
                    tapout("sg", tap_t["sg"][128 * m:128 * (m + 1), t0:t0 + TT],
                           s_scr[:, 0:TT], [("dv", N("dv"))])
            if "vt" in taps:
                for w_ in range(NW):
                    tapout("vt", tap_t["vt"][t0 + w_ * 128:t0 + (w_ + 1) * 128, :],
                           VT(w_, 0, D), [("dv", dv_vt)])
            if stage < 6:
                continue

            ret_state = {}
            for w in range(NW):
                gw = ti * NW + w
                wlo = w * 128
                for r4 in range(4):
                    for j in range(4):
                        m = r4 * 4 + j
                        OP("tensor", lambda e, m=m, j=j, wlo=wlo: e.transpose(
                            pDb[:, 128 * j:128 * (j + 1)], KF(m, wlo, 128),
                            s_identb[:, :]),
                           waits=[("dv", max(dv_rope, ret_state.get("ts_dv", 0))
                                   if (r4 == 0 and j == 0) else N("dv"))],
                           inc="pe")
                    pe_t4 = N("pe")
                    for j in range(4):
                        m = r4 * 4 + j
                        OP("vector", lambda e, m=m, j=j: e.tensor_scalar(
                            s_kt[:, 128 * m:128 * (m + 1)], pDb[:, 128 * j:128 * (j + 1)],
                            s_ksc[:, m // 2:m // 2 + 1], None, ALU.mult, ALU.bypass),
                           waits=[("pe", pe_t4)], inc="dv")
                dv_kt = N("dv")
                if "kt" in taps and ti == 0:
                    tapout("kt", tap_t["kt"][w * 128:(w + 1) * 128, :], s_kt[:, :],
                           [("dv", dv_kt)])
                dv_winstart = ret_state.get("rb_dv", 0)
                for h in range(H):
                    for a in range(2):
                        OP("tensor", lambda e, h=h, a=a, wlo=wlo: e.matmul(
                            pA[:, 128 * h:128 * (h + 1)], KF(2 * h + a, wlo, 128),
                            QF(2 * h + a, wlo, 128), start=(a == 0), stop=(a == 1)),
                           waits=[("dv", max(dv_rope, dv_winstart))], inc="pe")
                pe_A = N("pe")
                for h in range(H):
                    OP("vector", lambda e, h=h: e.tensor_tensor(
                        s_as[:, h * W:(h + 1) * W], pA[:, 128 * h:128 * (h + 1)],
                        s_dm[:, h * W:(h + 1) * W], ALU.mult),
                       waits=[("pe", pe_A)], inc="dv")
                dv_as = N("dv")
                for h in range(H):
                    for a in range(2):
                        b = 2 * h + a
                        pj, off = pp[b // 4], (b % 4) * 128
                        nmm = 1 + (2 if gw > 0 else 0)
                        OP("tensor", lambda e, h=h, a=a, w=w, pj=pj, off=off,
                           st=(gw == 0): e.matmul(
                            pj[:, off:off + 128], VT(w, 256 * h + 128 * a, 128),
                            s_as[:, h * W:(h + 1) * W], start=True, stop=st),
                           waits=[("dv", dv_as), bank_free[b // 4]], inc="pe")
                        if gw > 0:
                            for ap_ in range(2):
                                OP("tensor", lambda e, h=h, a=a, ap_=ap_, wlo=wlo, pj=pj,
                                   off=off: e.matmul(
                                        pj[:, off:off + 128],
                                        TBF(h, ap_, 128 * a, 128),
                                        QF(2 * h + ap_, wlo, 128),
                                        start=False, stop=(ap_ == 1)),
                                   waits=[("ac", N("ac"))], inc="pe")
                pe_o = N("pe")
                for h in range(H):
                    for a in range(2):
                        slot = (2 * h + a) % 2
                        OP("tensor", lambda e, h=h, a=a, w=w, slot=slot: e.matmul(
                            pD[:, 256 * slot:256 * slot + 256],
                            s_kt[:, (2 * h + a) * 128:(2 * h + a) * 128 + 128],
                            VT(w, 256 * h, 256), start=True, stop=True),
                           waits=[("dv", max(dv_kt, N("dv"))), ("ac", N("ac"))], inc="pe")
                        pe_ds = N("pe")
                        gW = GAMMA[h] ** W
                        OP("vector", lambda e, h=h, a=a, slot=slot, gW=gW:
                           e.scalar_tensor_tensor(
                                T32A(h, a), T32A(h, a), gW,
                                pD[:, 256 * slot:256 * slot + 256], ALU.mult, ALU.add),
                           waits=[("pe", pe_ds), ("dv", dv_tinit)], inc="dv")
                        ret_state["ts_dv"] = N("dv")
                        OP("scalar", lambda e, h=h, a=a: e.activation(
                            TBF(h, a), T32A(h, a), AF.Copy),
                           waits=[("dv", N("dv")), ("pe", pe_o)], inc="ac")
                        tb_cast[(h, a)] = N("ac")
                for m in range(NC):
                    b = m
                    pj, off = pp[b // 4], (b % 4) * 128
                    sq = m % 2
                    OP("scalar", lambda e, pj=pj, off=off, sq=sq: e.activation(
                        s_scr2[:, sq * 128:sq * 128 + 128], pj[:, off:off + 128],
                        AF.Square),
                       waits=[("pe", pe_o), ("pe", N("pe"))], inc="ac")
                    OP("tensor", lambda e, m=m, sq=sq: e.matmul(
                        pS[0:8, 0:128], s_sel8[:, m * 8:(m + 1) * 8],
                        s_scr2[:, sq * 128:sq * 128 + 128],
                        start=(m == 0), stop=(m == NC - 1)),
                       waits=[("ac", N("ac"))], inc="pe")
                pe_s2 = N("pe")
                OP("vector", lambda e: e.tensor_scalar(
                    s_rstd[:, :], pS[0:8, 0:128], 1.0 / 256.0, None, ALU.mult, ALU.bypass),
                   waits=[("pe", pe_s2)], inc="dv")
                OP("scalar", lambda e: e.activation(s_rstd[:, :], s_rstd[:, :], AF.Sqrt,
                                                    bias=EPS),
                   waits=[("dv", N("dv"))], inc="ac")
                OP("vector", lambda e: e.reciprocal(s_rstd[:, :], s_rstd[:, :]),
                   waits=[("ac", N("ac"))], inc="dv")
                dv_rstd = N("dv")
                for h in range(H):
                    OP("tensor", lambda e, h=h: e.matmul(
                        pA[:, 128 * h:128 * (h + 1)], s_selh[:, h * 128:(h + 1) * 128],
                        s_rstd[:, :], start=True, stop=True),
                       waits=[("dv", max(dv_rstd, dv_as))], inc="pe")
                pe_rb = N("pe")
                OP("vector", lambda e: e.tensor_copy(s_rb[:, :], pA[:, 0:H * 128]),
                   waits=[("pe", pe_rb)], inc="dv")
                dv_rb = N("dv")
                ret_state["rb_dv"] = dv_rb
                for m in range(NC):
                    h, a = m // 2, m % 2
                    b = m
                    pj, off = pp[b // 4], (b % 4) * 128
                    OP("vector", lambda e, pj=pj, off=off, h=h: e.tensor_tensor(
                        s_scr[:, 0:128], pj[:, off:off + 128],
                        s_rb[:, 128 * h:128 * (h + 1)], ALU.mult),
                       waits=[("dv", dv_rb)], inc="dv")
                    OP("vector", lambda e, a=a: e.tensor_scalar(
                        s_scr[:, 0:128], s_scr[:, 0:128], s_gnw[:, a:a + 1], None,
                        ALU.mult, ALU.bypass), inc="dv")
                    OP("vector", lambda e, m=m, wlo=wlo: e.tensor_tensor(
                        ATCH(m, wlo, 128), s_scr[:, 0:128], SGCH(m, wlo, 128), ALU.mult),
                       waits=[("ac", ac_sg)], inc="dv")
                    bank_free[b // 4] = ("dv", N("dv"))
            dv_og = N("dv")
            if "og" in taps:
                for m in range(NC):
                    OP("vector", lambda e, m=m: e.tensor_copy(s_scr[:, 0:TT], ATCH(m)),
                       waits=[("dv", dv_og)], inc="dv")
                    tapout("og", tap_t["og"][128 * m:128 * (m + 1), t0:t0 + TT],
                           s_scr[:, 0:TT], [("dv", N("dv"))])
            if "ts32" in taps and ti == NTILE - 1:
                tapout("ts32", tap_t["ts32"][:, :], s_t32[:, :], [("dv", dv_og)])

            if stage < 7:
                continue

            if stage >= 4:
                xsl_t = s_xsl[xrow:xrow + 32, gcol0:gcol0 + TT]
                OP("tensor", lambda e, prow=prow, ot=oht_tile: e.matmul(
                    pS[0:1, 0:TT], s_gmm[prow:prow + 64, 1:2], ot,
                    start=True, stop=True),
                   waits=[("gd", gd_gmm), ("pe", pe_rows)], inc="pe")
                OP("tensor", lambda e, prow=prow, ot=oht_tile: e.matmul(
                    pS[0:1, 256:256 + TT], s_gmm[prow:prow + 64, 2:3], ot,
                    start=True, stop=True), inc="pe")
                OP("tensor", lambda e, prow=prow, ot=oht_tile: e.matmul(
                    pD[0:1, 0:TT], s_osc[prow:prow + 64, 0:1], ot,
                    start=True, stop=True), inc="pe")
                pe_r3 = N("pe")
                OP("vector", lambda e: e.tensor_copy(R(3), pS[0:1, 0:TT]),
                   waits=[("pe", pe_r3)], inc="dv")
                OP("vector", lambda e: e.tensor_copy(R(4), pS[0:1, 256:256 + TT]), inc="dv")
                OP("vector", lambda e: e.tensor_copy(R(8), pD[0:1, 0:TT]), inc="dv")
                dv_r3 = N("dv")
                OP("tensor", lambda e: e.matmul(
                    pp[2][0:32, 0:TT], s_ones1[0:1, 0:32], R(3), start=True, stop=True),
                   waits=[("dv", dv_r3)], inc="pe")
                OP("tensor", lambda e: e.matmul(
                    pp[3][0:32, 0:TT], s_ones1[0:1, 0:32], R(4), start=True, stop=True),
                   inc="pe")
                OP("tensor", lambda e: e.matmul(
                    pD[0:32, 0:TT], s_ones1[0:1, 0:32], R(8), start=True, stop=True),
                   inc="pe")
                pe_b3 = N("pe")
                OP("vector", lambda e, xsl_t=xsl_t: e.tensor_tensor(
                    s_u16[0:32, :], xsl_t, pp[2][0:32, 0:TT], ALU.subtract),
                   waits=[("pe", pe_b3)], inc="dv")
                OP("vector", lambda e: e.tensor_tensor(
                    s_u16[0:32, :], s_u16[0:32, :], pp[3][0:32, 0:TT], ALU.mult), inc="dv")
                OP("vector", lambda e: e.tensor_scalar(
                    s_u16[0:32, :], s_u16[0:32, :], 0.0, 1.0 - 1e-6, ALU.max, ALU.min),
                   inc="dv")
                OP("vector", lambda e: e.tensor_scalar(
                    s_u16[0:32, :], s_u16[0:32, :], 16.0, None, ALU.mult, ALU.bypass),
                   inc="dv")
                dv_u16a = N("dv")
                OP("tensor", lambda e: e.matmul(
                    pp[0][:, 0:TT], s_prep[:, :], s_u16[0:32, :], start=True, stop=True),
                   waits=[("dv", dv_u16a)], inc="pe")
                pe_u16r = N("pe")
                OP("vector", lambda e: e.tensor_scalar(
                    s_msk[:, :], pp[0][:, 0:TT], s_gith[:, 0:1], None, ALU.is_ge,
                    ALU.bypass), waits=[("pe", pe_u16r)], inc="dv")
                for it in range(1, 4):
                    OP("vector", lambda e, it=it: e.tensor_scalar(
                        s_dl[:, :], pp[0][:, 0:TT], s_gith[:, it:it + 1], None, ALU.is_ge,
                        ALU.bypass), inc="dv")
                    OP("vector", lambda e: e.tensor_tensor(
                        s_msk[:, :], s_msk[:, :], s_dl[:, :], ALU.add), inc="dv")
                dv_gicmp = N("dv")
                OP("tensor", lambda e: e.matmul(
                    pS[0:32, 0:TT], s_rfold[:, :], s_msk[:, :], start=True, stop=True),
                   waits=[("dv", dv_gicmp)], inc="pe")
                OP("vector", lambda e: e.tensor_copy(s_gif[0:32, :], pS[0:32, 0:TT]),
                   waits=[("pe", N("pe"))], inc="dv")
                dv_u16 = N("dv")
                if "u16" in taps:
                    tapout("u16", tap_t["u16"][0:32, t0:t0 + TT], s_u16[0:32, :],
                           [("dv", dv_u16)])
                OP("tensor", lambda e: e.matmul(
                    pp[0][:, 0:TT], s_prep[:, :], s_gif[0:32, :], start=True, stop=True),
                   waits=[("dv", dv_u16)], inc="pe")
                pe_gir = N("pe")
                for g4 in range(4):
                    OP("tensor", lambda e, g4=g4, prow=prow, ot=oht_tile: e.matmul(
                        pp[1][:, 0:TT],
                        s_m1g[prow:prow + 64, 128 * g4:128 * (g4 + 1)],
                        ot, start=True, stop=True),
                       waits=[("dv", N("dv")), ("pe", pe_gir)], inc="pe")
                    OP("tensor", lambda e, g4=g4, prow=prow, ot=oht_tile: e.matmul(
                        pp[2][:, 0:TT],
                        s_m2g[prow:prow + 64, 128 * g4:128 * (g4 + 1)],
                        ot, start=True, stop=True), inc="pe")
                    pe_bst = N("pe")
                    OP("vector", lambda e, g4=g4: e.tensor_scalar(
                        s_msk[:, :], pp[0][:, 0:TT], s_gpp4[:, g4:g4 + 1], None,
                        ALU.is_equal, ALU.bypass),
                       waits=[("pe", pe_bst)], inc="dv")
                    if g4 == 0:
                        OP("vector", lambda e: e.tensor_tensor(
                            s_scr[:, 0:TT], s_msk[:, :], pp[1][:, 0:TT], ALU.mult), inc="dv")
                        OP("vector", lambda e: e.tensor_tensor(
                            s_scr[:, TT:2 * TT], s_msk[:, :], pp[2][:, 0:TT], ALU.mult), inc="dv")
                    else:
                        OP("vector", lambda e: e.tensor_tensor(
                            s_scr2[:, 0:TT], s_msk[:, :], pp[1][:, 0:TT], ALU.mult),
                           inc="dv")
                        OP("vector", lambda e: e.tensor_tensor(
                            s_scr[:, 0:TT], s_scr[:, 0:TT], s_scr2[:, 0:TT], ALU.add), inc="dv")
                        OP("vector", lambda e: e.tensor_tensor(
                            s_scr2[:, 0:TT], s_msk[:, :], pp[2][:, 0:TT], ALU.mult),
                           inc="dv")
                        OP("vector", lambda e: e.tensor_tensor(
                            s_scr[:, TT:2 * TT], s_scr[:, TT:2 * TT], s_scr2[:, 0:TT], ALU.add), inc="dv")
                dv_acc = N("dv")
                if "acc1" in taps:
                    tapout("acc1", tap_t["acc1"][:, t0:t0 + TT], s_scr[:, 0:TT],
                           [("dv", dv_acc)])
                    tapout("acc2", tap_t["acc2"][:, t0:t0 + TT], s_scr[:, TT:2 * TT],
                           [("dv", dv_acc)])
                if "girep" in taps:
                    OP("vector", lambda e: e.tensor_copy(s_msk[:, :], pp[0][:, 0:TT]),
                       waits=[("pe", pe_gir)], inc="dv")
                    tapout("girep", tap_t["girep"][:, t0:t0 + TT], s_msk[:, :],
                           [("dv", N("dv"))])
                OP("tensor", lambda e: e.matmul(
                    pS[0:32, 0:TT], s_rfold[:, :], s_scr[:, 0:TT], start=True, stop=True),
                   waits=[("dv", dv_acc)], inc="pe")
                OP("tensor", lambda e: e.matmul(
                    pS[0:32, 256:256 + TT], s_rfold[:, :], s_scr[:, TT:2 * TT],
                    start=True, stop=True), inc="pe")
                pe_fold = N("pe")
                OP("vector", lambda e: e.tensor_tensor(
                    s_scr2[0:32, 0:TT], s_u16[0:32, :], pS[0:32, 256:256 + TT], ALU.mult),
                   waits=[("pe", pe_fold)], inc="dv")
                OP("vector", lambda e: e.tensor_tensor(
                    s_scr2[0:32, 0:TT], s_scr2[0:32, 0:TT], pS[0:32, 0:TT], ALU.add),
                   inc="dv")
                OP("vector", lambda e: e.tensor_tensor(
                    s_scr2[0:32, 0:TT], s_scr2[0:32, 0:TT], pD[0:32, 0:TT], ALU.mult),
                   inc="dv")
                dv_delta = N("dv")
                if "delta" in taps:
                    tapout("delta", tap_t["delta"][0:32, t0:t0 + TT], s_scr2[0:32, 0:TT],
                           [("dv", dv_delta)])
                OP("tensor", lambda e: e.matmul(
                    pp[1][:, 0:TT], s_prep[:, :], s_scr2[0:32, 0:TT],
                    start=True, stop=True),
                   waits=[("dv", dv_delta)], inc="pe")
                OP("vector", lambda e: e.tensor_copy(s_dl[:, :], pp[1][:, 0:TT]),
                   waits=[("pe", N("pe"))], inc="dv")
                dv_dl = N("dv")
                ac_g = N("ac")

            sync_wslots()
            for m in range(NC):
                sl = m % 4
                bk = m % 4
                stream_w(w_b["wo"], NC, m, sl)
                for k in range(NC):
                    OP("tensor", lambda e, m=m, k=k, sl=sl, bk=bk: e.matmul(
                        pp[bk][:, 0:TT],
                        s_wb[:, sl * 2048 + 128 * k:sl * 2048 + 128 * (k + 1)],
                        ATCH(k), start=(k == 0), stop=(k == NC - 1)),
                       waits=([(f"ww{sl}", wslot_wd[sl]), ("dv", dv_og),
                               bank_free[bk]] if k == 0 else []),
                       inc="pe")
                wslot_pe[sl] = N("pe")
                OP("vector", lambda e, m=m, bk=bk: e.tensor_tensor(
                    XNc(m), pp[bk][:, 0:TT], XNc(m), ALU.add),
                   waits=[("pe", N("pe"))], inc="dv")
                bank_free[bk] = ("dv", N("dv"))
                OP("scalar", lambda e, m=m: e.activation(
                    s_scr[:, (m % 2) * TT:(m % 2) * TT + TT], XNc(m), AF.Square),
                   waits=[("dv", N("dv")), ("pe", N("pe") if m >= 2 else 0)], inc="ac")
                OP("tensor", lambda e, m=m: e.matmul(
                    pS[0:1, 0:TT], s_ones128[:, :],
                    s_scr[:, (m % 2) * TT:(m % 2) * TT + TT],
                    start=(m == 0), stop=(m == NC - 1)),
                   waits=[("ac", N("ac"))], inc="pe")
            dv_hh = N("dv")
            if "hh" in taps:
                for m in range(NC):
                    tapout("hh", tap_t["hh"][128 * m:128 * (m + 1), t0:t0 + TT], XNc(m),
                           [("dv", dv_hh)])

            pe_msq = N("pe")
            OP("vector", lambda e: e.tensor_scalar(
                R(9), pS[0:1, 0:TT], 1.0 / D, None, ALU.mult, ALU.bypass),
               waits=[("pe", pe_msq)], inc="dv")
            OP("scalar", lambda e: e.activation(R(9), R(9), AF.Sqrt, bias=EPS),
               waits=[("dv", N("dv"))], inc="ac")
            OP("vector", lambda e: e.reciprocal(R(9), R(9)), waits=[("ac", N("ac"))],
               inc="dv")
            dv_mr = N("dv")
            OP("tensor", lambda e: e.matmul(pS[:, 256:256 + TT], s_ones1[0:1, :], R(9),
                                            start=True, stop=True),
               waits=[("dv", dv_mr)], inc="pe")
            pe_mrb = N("pe")
            for m in range(NC):
                OP("vector", lambda e, m=m: e.tensor_tensor(
                    s_scr2[:, 0:TT], XNc(m), pS[:, 256:256 + TT], ALU.mult),
                   waits=[("pe", pe_mrb)], inc="dv")
                OP("vector", lambda e, m=m: e.tensor_scalar(
                    ATCH(m), s_scr2[:, 0:TT], s_mlpw[:, m:m + 1], None, ALU.mult,
                    ALU.bypass), inc="dv")
            dv_ym = N("dv")
            if "ym" in taps:
                for m in range(NC):
                    OP("vector", lambda e, m=m: e.tensor_copy(s_scr[:, 0:TT], ATCH(m)),
                       waits=[("dv", dv_ym)], inc="dv")
                    tapout("ym", tap_t["ym"][128 * m:128 * (m + 1), t0:t0 + TT],
                           s_scr[:, 0:TT], [("dv", N("dv"))])

            sync_wslots()
            for mg in range(NMG):
                sl = mg % 4
                bk = mg % 4
                stream_w(wgate_b, NC, mg, sl)
                for k in range(NC):
                    OP("tensor", lambda e, mg=mg, k=k, sl=sl, bk=bk: e.matmul(
                        pp[bk][:, 0:TT],
                        s_wb[:, sl * 2048 + 128 * k:sl * 2048 + 128 * (k + 1)],
                        ATCH(k), start=(k == 0), stop=(k == NC - 1)),
                       waits=([(f"ww{sl}", wslot_wd[sl]), ("dv", dv_ym),
                               bank_free[bk]] if k == 0 else []),
                       inc="pe")
                wslot_pe[sl] = N("pe")
                pe_g = N("pe")
                if mg < NMG // 2:
                    OP("scalar", lambda e, mg=mg, bk=bk: e.activation(
                        MLPA(mg), pp[bk][:, 0:TT], AF.Silu),
                       waits=[("pe", pe_g)], inc="ac")
                    bank_free[bk] = ("ac", N("ac"))
                else:
                    OP("vector", lambda e, mg=mg, bk=bk: e.tensor_tensor(
                        MLPA(mg - NMG // 2), pp[bk][:, 0:TT], MLPA(mg - NMG // 2),
                        ALU.mult),
                       waits=[("pe", pe_g), ("ac", N("ac"))], inc="dv")
                    bank_free[bk] = ("dv", N("dv"))
            dv_mlpa = N("dv"); ac_mlpa = N("ac")
            if "mlpact" in taps:
                for m in range(NKD):
                    OP("vector", lambda e, m=m: e.tensor_copy(
                        s_scr[:, 0:TT], MLPA(m)),
                       waits=[("dv", dv_mlpa), ("ac", ac_mlpa)], inc="dv")
                    tapout("mlpact", tap_t["mlpact"][128 * m:128 * (m + 1), t0:t0 + TT],
                           s_scr[:, 0:TT], [("dv", N("dv"))])

            ga_free = {0: ("dv", 0), 1: ("dv", 0)}
            OP("tensor", lambda e, RB=RB: e.matmul(
                pD[:, 0:TT], s_ones1[0:1, :], R(RB + 0), start=True, stop=True),
               waits=[("dv", N("dv")), ("ac", N("ac"))], inc="pe")
            OP("tensor", lambda e, RB=RB: e.matmul(
                pD[:, 256:256 + TT], s_ones1[0:1, :], R(RB + 2), start=True, stop=True),
               inc="pe")
            pe_xnb = N("pe")
            sync_wslots()
            for m in range(NC):
                sl = m % 2
                srcw = wdown_b[:, 128 * m:128 * (m + 1)].rearrange(
                    "(kc p) mm -> p kc mm", p=128)
                OP("sync", lambda e, s=srcw, sl=sl: e.dma_start(
                    out=s_wb[:, sl * INTER:sl * INTER + NKD * 128].rearrange(
                        "p (kc m) -> p kc m", m=128), in_=s),
                   waits=[("pe", wslot_pe[sl])] + cast_done,
                   inc=f"ww{sl}", inc_n=16)
                wslot_wd[sl] = N(f"ww{sl}")
                bk = m % 4
                for k in range(NKD):
                    OP("tensor", lambda e, m=m, k=k, sl=sl, bk=bk: e.matmul(
                        pp[bk][:, 0:TT],
                        s_wb[:, sl * INTER + 128 * k:sl * INTER + 128 * (k + 1)],
                        MLPA(k), start=(k == 0), stop=(k == NKD - 1)),
                       waits=([(f"ww{sl}", wslot_wd[sl]), ("dv", dv_mlpa),
                               ("ac", ac_mlpa), bank_free[bk]] if k == 0 else []),
                       inc="pe")
                wslot_pe[sl] = N("pe")
                pe_dn = N("pe")
                xsl_ = m % 2
                OP("gpsimd", lambda e, m=m, xsl_=xsl_, t0=t0: e.dma_start(
                    out=s_xc[:, xsl_ * TT:(xsl_ + 1) * TT],
                    in_=x_fm[128 * m:128 * (m + 1), t0:t0 + TT]),
                   waits=[("dv", xslot_dv[xsl_])], inc=f"wx{xsl_}", inc_n=16)
                xslot_wd[xsl_] = N(f"wx{xsl_}")
                osl = m % 2
                OP("vector", lambda e, m=m, bk=bk: e.tensor_tensor(
                    s_scr2[:, (m % 2) * TT:(m % 2) * TT + TT], pp[bk][:, 0:TT],
                    XNc(m), ALU.add),
                   waits=[("pe", pe_dn)], inc="dv")
                bank_free[bk] = ("dv", N("dv"))
                OP("tensor", lambda e, m=m, prow=prow, ot=ohtb_tile: e.matmul(
                    pA[:, (m % 2) * 512:(m % 2) * 512 + TT],
                    s_ek2[prow:prow + 64, 128 * m:128 * (m + 1)], ot,
                    start=True, stop=True),
                   waits=[ga_free[m % 2], ("ac", N("ac"))], inc="pe")
                pe_gm = N("pe")
                OP("vector", lambda e, m=m: e.tensor_tensor(
                    s_msk[:, :], pA[:, (m % 2) * 512:(m % 2) * 512 + TT], s_dl[:, :],
                    ALU.mult),
                   waits=[("pe", pe_gm)], inc="dv")
                ga_free[m % 2] = ("dv", N("dv"))
                OP("vector", lambda e, m=m, xsl_=xsl_: e.tensor_tensor(
                    s_gch[:, (m % 2) * TT:(m % 2) * TT + TT],
                    s_xc[:, xsl_ * TT:(xsl_ + 1) * TT], pD[:, 0:TT], ALU.mult),
                   waits=[(f"wx{xsl_}", xslot_wd[xsl_]), ("pe", pe_xnb)], inc="dv")
                OP("vector", lambda e, m=m: e.tensor_tensor(
                    s_gch[:, (m % 2) * TT:(m % 2) * TT + TT],
                    s_gch[:, (m % 2) * TT:(m % 2) * TT + TT], pD[:, 256:256 + TT],
                    ALU.add), inc="dv")
                OP("vector", lambda e, m=m: e.tensor_tensor(
                    s_msk[:, :], s_msk[:, :],
                    s_gch[:, (m % 2) * TT:(m % 2) * TT + TT], ALU.add), inc="dv")
                OP("scalar", lambda e, m=m: e.activation(
                    s_gch[:, (m % 2) * TT:(m % 2) * TT + TT], s_msk[:, :], AF.Silu),
                   waits=[("dv", N("dv"))], inc="ac")
                ac_gm = N("ac")
                if "sfin" in taps:
                    OP("vector", lambda e, m=m: e.tensor_copy(
                        s_scr[:, 0:TT], s_scr2[:, (m % 2) * TT:(m % 2) * TT + TT]),
                       inc="dv")
                    tapout("sfin", tap_t["sfin"][128 * m:128 * (m + 1), t0:t0 + TT],
                           s_scr[:, 0:TT], [("dv", N("dv"))])
                OP("vector", lambda e, m=m: e.tensor_tensor(
                    s_scr2[:, (m % 2) * TT:(m % 2) * TT + TT],
                    s_scr2[:, (m % 2) * TT:(m % 2) * TT + TT],
                    s_gch[:, (m % 2) * TT:(m % 2) * TT + TT], ALU.mult),
                   waits=[("ac", ac_gm)], inc="dv")
                OP("vector", lambda e, m=m, osl=osl, xsl_=xsl_: e.tensor_tensor(
                    s_oc[:, osl * TT:(osl + 1) * TT],
                    s_scr2[:, (m % 2) * TT:(m % 2) * TT + TT],
                    s_xc[:, xsl_ * TT:(xsl_ + 1) * TT], ALU.add),
                   waits=[(f"wx{xsl_}", xslot_wd[xsl_]), (f"od{osl}", oslot_od[osl])],
                   inc="dv")
                xslot_dv[xsl_] = N("dv")
                oslot_dv[osl] = N("dv")
                OP("scalar", lambda e, m=m, osl=osl, t0=t0: e.dma_start(
                    out=out_fm[128 * m:128 * (m + 1), t0:t0 + TT],
                    in_=s_oc[:, osl * TT:(osl + 1) * TT]),
                   waits=[("dv", oslot_dv[osl])], inc=f"od{osl}", inc_n=16)
                oslot_od[osl] = N(f"od{osl}")

    if stage < 7:
        OP("vector", lambda e: e.memset(s_oc[:, 0:TT], 0.0),
           waits=[("dv", N("dv"))], inc="dv")
        dvz = N("dv")
        for m in range(NC):
            for r in range(NTILE):
                OP("scalar", lambda e, m=m, r=r: e.dma_start(
                    out=out_fm[128 * m:128 * (m + 1), r * TT:(r + 1) * TT],
                    in_=s_oc[:, 0:TT]),
                   waits=[("dv", dvz)], inc="od0", inc_n=16)

    OP("gpsimd", lambda e: e.memset(s_cc[0:1, 0:2], 0.0),
       waits=[(s, P.now(s)) for s in
              ["od0", "od1", "wc", "wxt", "wx0", "wx1", "ww0", "ww1", "ww2", "ww3", "wg0",
               "wg1", "wg2", "wg3",
               "wr", "ci0", "ci1", "ci2", "co0", "co1", "co2", "gd", "dv", "ac",
               "pe"]], inc="gp")

    P.emit(nc, sems)
    es.close()
    return nc, sorted(tap_t.keys())


_CACHE = {}


def make_inputs(inputs, taps=()):
    con = host_constants()
    osc = np.asarray(inputs["L0_ffn_output_scale"], np.float32).reshape(64, 1)
    base = {
        "wq": np.asarray(inputs["L0_attn_Wq"], np.float32),
        "wk": np.asarray(inputs["L0_attn_Wk"], np.float32),
        "wv": np.asarray(inputs["L0_attn_Wv"], np.float32),
        "wg": np.asarray(inputs["L0_attn_Wg"], np.float32),
        "wo": np.asarray(inputs["L0_attn_Wo"], np.float32),
        "wgate": np.asarray(inputs["L0_attn_W_gate"], np.float32),
        "wdown": np.asarray(inputs["L0_attn_W_down"], np.float32),
        "diags": np.concatenate([
            diag_layout(np.asarray(inputs["L0_attn_conv_wq"], np.float32)),
            diag_layout(np.asarray(inputs["L0_attn_conv_wk"], np.float32)),
            diag_layout(np.asarray(inputs["L0_attn_conv_wv"], np.float32))], axis=0),
        "attn_nw": np.asarray(inputs["L0_attn_attn_norm_w"], np.float32).reshape(1, D),
        "gnorm_w": np.asarray(inputs["L0_attn_gnorm_w"], np.float32).reshape(1, DK),
        "mlp_nw": np.asarray(inputs["L0_attn_mlp_norm_w"], np.float32).reshape(1, D),
        "ln_g": np.asarray(inputs["L0_ffn_ln_gamma"], np.float32).reshape(1, D),
        "ln_b": np.asarray(inputs["L0_ffn_ln_beta"], np.float32).reshape(1, D),
        "spl_b": np.asarray(inputs["L0_ffn_spline_bases"], np.float32).reshape(64, -1),
        "spl_s": np.asarray(inputs["L0_ffn_spline_slopes"], np.float32).reshape(64, -1),
        "osc_dup": np.concatenate([osc, osc], axis=0),
    }
    base.update({k: np.ascontiguousarray(v) for k, v in con.items()})
    x = np.asarray(inputs["L0_attn_hidden_states"], np.float32)
    maps = []
    for c in range(NCORE):
        m = dict(base)
        m["x_fm"] = np.ascontiguousarray(x[c].T)
        maps.append(m)
    return maps


def kernel(**inputs):
    if "nc" not in _CACHE:
        _CACHE["nc"] = build(stage=9)[0]
    nc = _CACHE["nc"]
    in_maps = make_inputs(inputs)
    res = run_bass_kernel_spmd(nc, in_maps, core_ids=list(range(NCORE)))
    return np.stack([np.ascontiguousarray(r["out_fm"].T) for r in res.results], axis=0)
